# version 09
# baseline (speedup 1.0000x reference)
import numpy as np
import concourse.bass as bass
import concourse.mybir as mybir
from concourse.bass_utils import run_bass_kernel_spmd
from contextlib import ExitStack

F32 = mybir.dt.float32
BF16 = mybir.dt.bfloat16
AF = mybir.ActivationFunctionType
ALU = mybir.AluOpType

NT = 2048
NTILE = 16
G = 4
EPS = 1e-5
SCALE = 0.125


class Sem:
    def __init__(self, nc, stack, name):
        self.h = stack.enter_context(nc.semaphore(name))
        self.n = 0

    def bump(self, k=1):
        self.n += k
        return self.n


class Prog:
    ENGS = ("sync", "gpsimd", "tensor", "scalar", "vector")

    def __init__(self):
        self.qs = {e: [] for e in self.ENGS}

    def emit(self, eng, fn):
        self.qs[eng].append(fn)

    def wait(self, eng, sem, n):
        if n > 0:
            self.emit(eng, lambda e, h=sem.h, n=n: e.wait_ge(h, n))

    def run(self, block):
        qs = self.qs

        @block.sync
        def _(e):
            for fn in qs["sync"]:
                fn(e)

        @block.gpsimd
        def _(e):
            for fn in qs["gpsimd"]:
                fn(e)

        @block.tensor
        def _(e):
            for fn in qs["tensor"]:
                fn(e)

        @block.scalar
        def _(e):
            for fn in qs["scalar"]:
                fn(e)

        @block.vector
        def _(e):
            for fn in qs["vector"]:
                fn(e)


def build_nc():
    nc = bass.Bass("TRN2", target_bir_lowering=False)
    st = ExitStack()

    x_hbm = nc.dram_tensor("x", [NT, 1024], F32, kind="ExternalInput")
    mq_h = nc.dram_tensor("mq", [1024, 512], F32, kind="ExternalInput")
    mk_h = nc.dram_tensor("mk", [1024, 512], F32, kind="ExternalInput")
    mv_h = nc.dram_tensor("mv", [1024, 1024], F32, kind="ExternalInput")
    wrt_h = nc.dram_tensor("wrt", [1024, 1024], F32, kind="ExternalInput")
    wkt_h = nc.dram_tensor("wkt", [1024, 1024], F32, kind="ExternalInput")
    wvt_h = nc.dram_tensor("wvt", [1024, 1024], F32, kind="ExternalInput")
    mur_h = nc.dram_tensor("murep", [128, 8, 128], F32, kind="ExternalInput")
    muk_h = nc.dram_tensor("mukrep", [128, 8, 128], F32, kind="ExternalInput")
    sel_h = nc.dram_tensor("sel", [8, 1], F32, kind="ExternalInput")
    y_hbm = nc.dram_tensor("y", [NT, 1024], F32, kind="ExternalOutput")
    cc_in = nc.dram_tensor("cc_in", [1, 1024], F32)
    cc_out = nc.dram_tensor("cc_out", [8, 1024], F32)

    def sb(name, shape, dtype):
        return st.enter_context(nc.sbuf_tensor(name, shape, dtype))

    x2 = sb("x2", [128, NTILE, 1024], F32)

    warena = nc.alloc_sbuf_tensor("warena", [128, 49152], mybir.dt.uint8)
    wbase = nc.lookup_mloc(warena).addr

    def at(name, shape, dtype, off):
        return nc.alloc_sbuf_tensor_at(name, shape, dtype, offset=wbase + off)

    mq = at("mq_sb", [128, 8, 512], BF16, 0)
    mk = at("mk_sb", [128, 8, 512], BF16, 8192)
    mv = at("mv_sb", [128, 8, 1024], BF16, 16384)
    wrt = at("wrt_sb", [128, 8, 1024], BF16, 0)
    wkt = at("wkt_sb", [128, 8, 1024], BF16, 16384)
    wvt = at("wvt_sb", [128, 8, 1024], BF16, 32768)

    tarena = nc.alloc_sbuf_tensor("tarena", [128, 81984], mybir.dt.uint8)
    tbase = nc.lookup_mloc(tarena).addr

    def ta(name, shape, dtype, off):
        return nc.alloc_sbuf_tensor_at(name, shape, dtype, offset=tbase + off)

    n1 = ta("n1_sb", [128, 4, 1024], BF16, 0)
    xt = ta("xt_sb", [128, 8, 512], BF16, 8192)
    qt = ta("qt_sb", [128, 4, 512], BF16, 16384)
    ktt = ta("ktt_sb", [128, 4, 512], BF16, 20480)
    ktd = ta("ktd_sb", [128, 8, 512], BF16, 24576)
    v2 = ta("v2_sb", [128, 8, 4, 128], BF16, 32768)
    e_sb = ta("e_sb", [128, 2, 4, 1024], BF16, 40960)
    dp = ta("dp_sb", [128, 2, 4], F32, 57344)
    dr = ta("dr_sb", [128, 2, 2], F32, 57472)
    lno = ta("lno_sb", [128, 4, 1024], BF16, 57600)
    n2t = ta("n2t_sb", [128, 8, 2048], BF16, 0)
    n2h = ta("n2h_sb", [128, 8, 1], BF16, 32768)
    n2tok = ta("n2tok_sb", [128, 2, 1024], BF16, 32800)
    xr4 = ta("xr4_sb", [128, 4, 8, 128], BF16, 36896)
    xk4 = ta("xk4_sb", [128, 4, 8, 128], BF16, 45088)
    t1b = ta("t1_sb", [128, 8, 128], BF16, 53280)
    r4 = ta("r4_sb", [128, 4, 1024], BF16, 55328)
    kf4 = ta("kf4_sb", [128, 8, 512], BF16, 63520)
    krelu = ta("krelu_sb", [128, 512], F32, 71712)
    vo = ta("vo_sb", [128, 2, 1024], F32, 73760)

    murep = sb("murep_sb", [128, 8, 128], BF16)
    mukrep = sb("mukrep_sb", [128, 8, 128], BF16)
    o_sb = sb("o_sb", [128, 4, 1024], BF16)
    cc_sb = sb("cc_sb", [8, 1024], BF16)
    sel_sb = sb("sel_sb", [8, 1], BF16)
    st12 = sb("st12_sb", [128, 4, 12], F32)
    ag = sb("ag_sb", [128, 4, 2, 2], F32)
    lnt = sb("lnt_sb", [128, 5, 4], F32)
    rvbuf = nc.alloc_sbuf_tensor("rvbuf", [128, 64], mybir.dt.uint8)
    rvbase = nc.lookup_mloc(rvbuf).addr
    rv_f = nc.alloc_sbuf_tensor_at("rv_f", [128, 16], F32, offset=rvbase)
    rv_i = nc.alloc_sbuf_tensor_at("rv_i", [128, 16], mybir.dt.int32,
                                   offset=rvbase)
    rsig = nc.alloc_sbuf_tensor_at("rsig_v", [128, 4], F32,
                                   offset=rvbase + 32)
    magic = sb("magic_sb", [128, 4], mybir.dt.int32)
    rtm = sb("rtm_sb", [128, 4], F32)
    rsA = sb("rsA_sb", [128, 16], F32)
    mnA = sb("mnA_sb", [128, 16], F32)

    ps = st.enter_context(nc.psum_tensor("ps", [128, 8, 512], F32))
    PAIRS = ((0, 1), (6, 7))

    sems = {}
    for name in ("sx", "sw", "stp", "ssw", "smm", "sxp", "svc", "sst", "scc",
                 "sg"):
        sems[name] = Sem(nc, st, name)
    sx, sw, stp, ssw = sems["sx"], sems["sw"], sems["stp"], sems["ssw"]
    smm, sxp, svc, sst = sems["smm"], sems["sxp"], sems["svc"], sems["sst"]
    scc, sg = sems["scc"], sems["sg"]

    P = Prog()

    def dma(eng, sem, out_ap, in_ap, transpose=False, slow=False):
        n = sem.bump(16)
        P.emit(eng, lambda e, o=out_ap, i=in_ap, h=sem.h, t=transpose, s=slow:
               e.dma_start(out=o, in_=i, transpose=t,
                           allow_slow_non_contiguous=s).then_inc(h, 16))
        return n

    def mm(out_ap, lhsT, rhs, start, stop, tp=None, inc=None):
        if inc is not None:
            n = inc.bump(1)
            P.emit("tensor", lambda e, o=out_ap, l=lhsT, r=rhs, a=start, b=stop,
                   t=tp, h=inc.h:
                   e.matmul(o, l, r, start=a, stop=b, tile_position=t,
                            skip_group_check=True).then_inc(h, 1))
            return n
        P.emit("tensor", lambda e, o=out_ap, l=lhsT, r=rhs, a=start, b=stop,
               t=tp:
               e.matmul(o, l, r, start=a, stop=b, tile_position=t,
                        skip_group_check=True))
        return None

    def vop(meth, *args, inc=None, **kw):
        if inc is not None:
            n = inc.bump(1)
            P.emit("vector", lambda e, m=meth, a=args, k=kw, h=inc.h:
                   getattr(e, m)(*a, **k).then_inc(h, 1))
            return n
        P.emit("vector", lambda e, m=meth, a=args, k=kw: getattr(e, m)(*a, **k))
        return None

    def sop(meth, *args, inc=None, **kw):
        if inc is not None:
            n = inc.bump(1)
            P.emit("scalar", lambda e, m=meth, a=args, k=kw, h=inc.h:
                   getattr(e, m)(*a, **k).then_inc(h, 1))
            return n
        P.emit("scalar", lambda e, m=meth, a=args, k=kw: getattr(e, m)(*a, **k))
        return None

    dma("sync", sx, x2[:, :, :],
        bass.AP(x_hbm, 0, [[1024, 128], [128 * 1024, NTILE], [1, 1024]]))
    for dst, src, nch, w in ((mq, mq_h, 8, 512), (mk, mk_h, 8, 512),
                             (mv, mv_h, 8, 1024)):
        dma("gpsimd", sw, dst[:, :, :],
            bass.AP(src, 0, [[w, 128], [128 * w, nch], [1, w]]))
    dma("gpsimd", sw, wvt[:, :, :],
        bass.AP(wvt_h, 0, [[1024, 128], [128 * 1024, 8], [1, 1024]]))
    dma("gpsimd", sw, murep[:, :, :], mur_h[:, :, :])
    dma("gpsimd", sw, mukrep[:, :, :], muk_h[:, :, :])
    dma("gpsimd", sw, sel_sb[:, :], sel_h[:, :])
    n_w_init = sw.n
    P.emit("vector", lambda e: e.memset(magic[:, :], 0x5F3759DF))

    def ln_stats(srcs, wait_evt=None):
        if wait_evt is not None:
            P.wait("vector", *wait_evt)
        for i, s in enumerate(srcs):
            vop("bn_stats", st12[:, i, 0:6], s[0])
            inc = svc if i == len(srcs) - 1 else None
            vop("bn_stats", st12[:, i, 6:12], s[1], inc=inc)
        P.wait("vector", svc, svc.n)
        for i in range(len(srcs)):
            vop("bn_aggr", ag[:, i, 0, :], st12[:, i, 0:6])
            inc = svc if i == len(srcs) - 1 else None
            vop("bn_aggr", ag[:, i, 1, :], st12[:, i, 6:12], inc=inc)
        P.wait("vector", svc, svc.n)
        k = len(srcs)
        vop("tensor_add", lnt[:, 0, 0:k], ag[:, 0:k, 0, 0], ag[:, 0:k, 1, 0])
        vop("tensor_sub", lnt[:, 1, 0:k], ag[:, 0:k, 0, 0], ag[:, 0:k, 1, 0])
        vop("tensor_add", lnt[:, 2, 0:k], ag[:, 0:k, 0, 1], ag[:, 0:k, 1, 1],
            inc=svc)
        P.wait("vector", svc, svc.n)
        vop("tensor_mul", lnt[:, 1, 0:k], lnt[:, 1, 0:k], lnt[:, 1, 0:k])
        vop("tensor_scalar_mul", lnt[:, 4, 0:k], lnt[:, 0, 0:k], -0.5, inc=svc)
        P.wait("vector", svc, svc.n)
        vop("scalar_tensor_tensor", lnt[:, 3, 0:k], lnt[:, 1, 0:k], 0.5,
            lnt[:, 2, 0:k], ALU.mult, ALU.add, inc=svc)
        P.wait("vector", svc, svc.n)
        vop("tensor_scalar", rv_f[:, 0:k], lnt[:, 3, 0:k], 0.5, EPS,
            ALU.mult, ALU.add, inc=svc)
        P.wait("vector", svc, svc.n)
        vop("tensor_scalar", rv_i[:, 8:8 + k], rv_i[:, 0:k], 1, None,
            ALU.arith_shift_right, inc=svc)
        P.wait("vector", svc, svc.n)
        vop("tensor_sub", rv_i[:, 8:8 + k], magic[:, 0:k], rv_i[:, 8:8 + k],
            inc=svc)
        P.wait("vector", svc, svc.n)
        for _ in range(3):
            vop("tensor_mul", rtm[:, 0:k], rsig[:, 0:k], rsig[:, 0:k],
                inc=svc)
            P.wait("vector", svc, svc.n)
            vop("tensor_mul", rtm[:, 0:k], rtm[:, 0:k], rv_f[:, 0:k],
                inc=svc)
            P.wait("vector", svc, svc.n)
            vop("tensor_scalar", rtm[:, 0:k], rtm[:, 0:k], -0.5, 1.5,
                ALU.mult, ALU.add, inc=svc)
            P.wait("vector", svc, svc.n)
            vop("tensor_mul", rsig[:, 0:k], rsig[:, 0:k], rtm[:, 0:k],
                inc=svc)
            P.wait("vector", svc, svc.n)
        return svc.n

    def vnorm(out_ap, src_ap, i, inc=None):
        return vop("tensor_scalar", out_ap, src_ap, lnt[:, 4, i:i + 1],
                   rsig[:, i:i + 1], ALU.add, ALU.mult, inc=inc)

    state = {
        "proj_mm": 0,
        "attnln_vc": 0,
        "otp": 0,
        "Bevac_vc": 0,
        "xt_tp": {},
        "exp_pair": [0, 0],
    }

    def emit_n1ln(g, first=False):
        tiles = [4 * g + i for i in range(4)]
        ln_stats([(x2[:, t, 0:512], x2[:, t, 512:1024]) for t in tiles],
                 wait_evt=(sx, 16) if first else None)
        for i in range(4):
            inc = svc if i == 3 else None
            vnorm(n1[:, i, :], x2[:, tiles[i], :], i, inc=inc)
        n_n1 = svc.n
        P.wait("sync", svc, n_n1)
        P.wait("sync", smm, state["proj_mm"])
        for i in range(4):
            dma("sync", stp, xt[:, :, 128 * i:128 * (i + 1)], n1[:, i, :],
                transpose=True)
        state["xt_tp"][g] = stp.n

    def emit_proj(g):
        P.wait("tensor", stp, state["xt_tp"][g])
        if g == 0:
            P.wait("tensor", sw, n_w_init)
        P.wait("tensor", svc, state["Bevac_vc"])
        P.wait("tensor", stp, state["otp"])
        qslots = (0, 1, 6, 7)
        if g > 0:
            P.wait("tensor", sxp, max(state["exp_pair"]))
        for jh2 in range(4):
            for dc in range(8):
                inc = smm if dc == 7 else None
                mm(ps[:, qslots[jh2], :], mq[:, dc, 128 * jh2:128 * (jh2 + 1)],
                   xt[:, dc, :], start=(dc == 0), stop=(dc == 7), inc=inc)
        n_qmm = smm.n
        for jh2 in range(4):
            for dc in range(8):
                inc = smm if dc == 7 else None
                mm(ps[:, 2 + jh2, :], mk[:, dc, 128 * jh2:128 * (jh2 + 1)],
                   xt[:, dc, :], start=(dc == 0), stop=(dc == 7), inc=inc)
        n_kmm = smm.n
        P.wait("vector", smm, n_qmm)
        for jh2 in range(4):
            vop("tensor_copy", qt[:, jh2, :], ps[:, qslots[jh2], :])
        P.wait("vector", smm, n_kmm)
        for jh2 in range(4):
            inc = svc if jh2 == 3 else None
            vop("tensor_copy", ktt[:, jh2, :], ps[:, 2 + jh2, :], inc=inc)
        P.wait("tensor", svc, svc.n)
        n_qk_ev = svc.n
        P.wait("sync", svc, n_qk_ev)
        dma("sync", ssw, bass.AP(ktd, 0, [[8 * 512, 64], [2 * 512, 4],
                                          [1, 512]]), ktt[0:64, :, :])
        dma("sync", ssw, bass.AP(ktd, 512, [[8 * 512, 64], [2 * 512, 4],
                                            [1, 512]]), ktt[64:128, :, :])
        n_swap1 = ssw.n
        P.wait("sync", ssw, n_swap1)
        dma("sync", ssw, ktd[64:128, :, :], ktd[0:64, :, :])
        n_swap = ssw.n
        return {"swap": n_swap, "qkev": n_qk_ev}

    def emit_vproj(g, cb):
        bA, bB = (2, 3) if cb % 2 == 0 else (4, 5)
        if cb >= 2:
            P.wait("tensor", svc, svc.n)
        for dc in range(8):
            for hh, bank in ((0, bA), (1, bB)):
                inc = smm if (dc == 7 and hh == 1) else None
                mm(ps[:, bank, :],
                   xt[:, dc, 128 * cb:128 * (cb + 1)],
                   mv[:, dc, 512 * hh:512 * (hh + 1)],
                   start=(dc == 0), stop=(dc == 7), inc=inc)
        P.wait("vector", smm, smm.n)
        vop("tensor_copy", v2[:, 0:4, cb, :],
            bass.AP(ps, 512 * bA, [[8 * 512, 128], [128, 4], [1, 128]]))
        vop("tensor_copy", v2[:, 4:8, cb, :],
            bass.AP(ps, 512 * bB, [[8 * 512, 128], [128, 4], [1, 128]]),
            inc=svc)

    def sc_dual(ci, q2, jhk, cb, accum):
        pr = q2 % 2
        a, b = PAIRS[pr]
        P.wait("tensor", sxp, state["exp_pair"][pr])
        mm(ps[:, a, :], ktd[0:64, jhk, 128 * cb:128 * (cb + 1)],
           qt[0:64, q2, :], start=True, stop=True, tp=(0, 0))
        n_dual = mm(ps[:, b, :], ktd[64:128, jhk, 128 * cb:128 * (cb + 1)],
                    qt[64:128, q2, :], start=True, stop=True, tp=(64, 0),
                    inc=smm)
        P.emit("scalar", lambda e, h=smm.h, n=n_dual: e.wait_ge(h, n))
        kw = {}
        if accum:
            kw["accum_out"] = dp[:, ci % 2, q2:q2 + 1]
        sop("activation", e_sb[:, ci % 2, q2, :],
            bass.AP(ps, 512 * a, [[8 * 512, 128], [512 * (b - a), 2],
                                  [1, 512]]),
            AF.Exp, scale=SCALE, inc=sxp, **kw)
        state["exp_pair"][pr] = sxp.n

    def o_mm(pci, jhq, first, last, inc=None):
        pjhk, pcb = divmod(pci, 4)
        slot = 2 + (jhq % 4)
        return mm(ps[:, slot, :], v2[:, pjhk, pcb, :],
                  e_sb[:, pci % 2, jhq // 2,
                       512 * (jhq % 2):512 * (jhq % 2) + 512],
                  start=first, stop=last, inc=inc)

    def emit_phaseA(g, pj):
        P.wait("tensor", ssw, pj["swap"])
        P.wait("tensor", svc, pj["qkev"])
        n_vev = {}
        n_o_done = {}
        n_vp = {}
        first_sc_wait = True
        for ci in range(32):
            jhk, cb = divmod(ci, 4)
            for q2 in range(4):
                if first_sc_wait:
                    P.wait("scalar", stp, state["otp"])
                    first_sc_wait = False
                if ci >= 2 and q2 == 0:
                    P.wait("scalar", smm, n_o_done[ci - 2])
                sc_dual(ci, q2, jhk, cb, accum=True)
                if (ci, q2) in ((0, 1), (0, 3), (1, 0), (1, 1)):
                    emit_vproj(g, 2 * ci + (q2 if ci == 1 else (q2 - 1) // 2))
                    n_vev[0] = svc.n
                    if ci == 1 and q2 == 1:
                        state["proj_mm"] = smm.n
                if ci > 0:
                    if q2 == 2:
                        P.wait("tensor", svc, n_vp[ci - 1])
                        if ci == 1:
                            P.wait("tensor", svc, n_vev[0])
                    if q2 >= 2:
                        for jj in range(2):
                            jhq = 2 * (q2 - 2) + jj
                            inc = smm if jhq == 3 else None
                            o_mm(ci - 1, jhq, first=(ci - 1 == 0),
                                 last=(ci - 1 == 31), inc=inc)
                        if q2 == 3:
                            n_o_done[ci - 1] = smm.n
            P.wait("vector", sxp, state["exp_pair"][1])
            ring = ci % 2
            vop("tensor_reduce", dr[:, ring, 0:1], dp[:, ring, :],
                mybir.AxisListType.X, ALU.add, inc=svc)
            P.wait("vector", svc, svc.n)
            vop("reciprocal", dr[:, ring, 1:2], dr[:, ring, 0:1], inc=svc)
            P.wait("vector", svc, svc.n)
            vop("tensor_scalar_mul", v2[:, jhk, cb, :], v2[:, jhk, cb, :],
                dr[:, ring, 1:2], inc=svc)
            n_vp[ci] = svc.n
        P.wait("tensor", sxp, state["exp_pair"][1])
        P.wait("tensor", svc, n_vp[31])
        for jhq in range(4):
            inc = smm if jhq == 3 else None
            o_mm(31, jhq, first=False, last=True, inc=inc)
        n_oA = smm.n
        P.wait("vector", smm, n_oA)
        for jhq in range(4):
            inc = svc if jhq == 3 else None
            vop("tensor_copy",
                e_sb[:, 0, jhq // 2, 512 * (jhq % 2):512 * (jhq % 2) + 512],
                ps[:, 2 + jhq, :], inc=inc)
        n_oev = svc.n
        P.wait("sync", svc, n_oev)
        for jhq in range(4):
            dma("sync", stp, o_sb[:, :, 128 * jhq:128 * (jhq + 1)],
                e_sb[:, 0, jhq // 2,
                     512 * (jhq % 2):512 * (jhq % 2) + 512], transpose=True)
        return {"oev": n_oev, "otpA": stp.n}

    def emit_phaseB(g, pa):
        n_oB_done = {}
        for ci in range(32):
            jhk, cb = divmod(ci, 4)
            for q2 in (2, 3):
                if ci >= 2 and q2 == 2:
                    P.wait("scalar", smm, n_oB_done[ci - 2])
                sc_dual(ci, q2, jhk, cb, accum=False)
                if ci > 0:
                    if q2 == 2 and ci == 1:
                        P.wait("tensor", svc, pa["oev"])
                    for jj in range(2):
                        jhq = 4 + 2 * (q2 - 2) + jj
                        inc = smm if jhq == 7 else None
                        o_mm(ci - 1, jhq, first=(ci - 1 == 0),
                             last=(ci - 1 == 31), inc=inc)
                    if q2 == 3:
                        n_oB_done[ci - 1] = smm.n
        P.wait("tensor", sxp, state["exp_pair"][1])
        for jhq in range(4, 8):
            inc = smm if jhq == 7 else None
            o_mm(31, jhq, first=False, last=True, inc=inc)
        n_oB = smm.n
        P.wait("vector", smm, n_oB)
        P.wait("vector", stp, pa["otpA"])
        for jhq in range(4, 8):
            j = jhq - 4
            inc = svc if j == 3 else None
            vop("tensor_copy",
                e_sb[:, 0, j // 2, 512 * (j % 2):512 * (j % 2) + 512],
                ps[:, 2 + j, :], inc=inc)
        state["Bevac_vc"] = svc.n
        P.wait("sync", svc, state["Bevac_vc"])
        for jhq in range(4, 8):
            j = jhq - 4
            dma("sync", stp, o_sb[:, :, 128 * jhq:128 * (jhq + 1)],
                e_sb[:, 0, j // 2, 512 * (j % 2):512 * (j % 2) + 512],
                transpose=True)
        state["otp"] = stp.n

    def emit_attnln(g):
        tiles = [4 * g + i for i in range(4)]
        P.wait("vector", stp, state["otp"])
        P.wait("vector", scc, scc.n)
        ln_stats([(o_sb[:, i, 0:512], o_sb[:, i, 512:1024])
                  for i in range(4)])
        for i in range(4):
            inc = svc if i == 3 else None
            vnorm(lno[:, i, :], o_sb[:, i, :], i, inc=inc)
        P.wait("vector", svc, svc.n)
        for i, t in enumerate(tiles):
            inc = svc if i == 3 else None
            vop("tensor_add", x2[:, t, :], x2[:, t, :], lno[:, i, :], inc=inc)
        P.wait("vector", svc, svc.n)
        state["attnln_vc"] = svc.n

    def emit_slab_stats(si):
        slab = [4 * si + i for i in range(4)]
        ln_stats([(x2[:, t, 0:512], x2[:, t, 512:1024]) for t in slab])
        vop("tensor_mul", mnA[:, 4 * si:4 * si + 4], lnt[:, 4, 0:4],
            rsig[:, 0:4])
        vop("tensor_copy", rsA[:, 4 * si:4 * si + 4], rsig[:, 0:4], inc=svc)
        P.wait("vector", svc, svc.n)
        return svc.n

    def emit_boundary_send():
        ln_stats([(x2[:, 15, 0:512], x2[:, 15, 512:1024])])
        vnorm(lno[:, 0, :], x2[:, 15, :], 0, inc=svc)
        n_bs = svc.n
        P.emit("gpsimd", lambda e, h=svc.h, n=n_bs: e.wait_ge(h, n))
        dma("gpsimd", scc, cc_in[:, :], lno[127:128, 0, :])
        P.wait("gpsimd", scc, scc.n)
        ncc = scc.bump(1)
        P.emit("gpsimd", lambda e, h=scc.h:
               e.collective_compute(
                   "AllGather", ALU.bypass,
                   replica_groups=[list(range(8))],
                   ins=[cc_in.ap().opt()],
                   outs=[cc_out.ap().opt()]).then_inc(h, 1))
        P.wait("gpsimd", scc, ncc)
        dma("gpsimd", scc, cc_sb[:, :], cc_out[:, :])

    ORDER = [3, 0, 1, 2]
    emit_n1ln(ORDER[0], first=True)
    pj = emit_proj(ORDER[0])
    for gi, g in enumerate(ORDER):
        pa = emit_phaseA(g, pj)
        if gi + 1 < G:
            emit_n1ln(ORDER[gi + 1])
        if gi == 3:
            emit_slab_stats(0)
            emit_slab_stats(1)
            emit_slab_stats(3)
            n_slab_stats013 = svc.n
        emit_phaseB(g, pa)
        if gi == 3:
            P.wait("tensor", scc, scc.n)
            P.wait("tensor", sw, n_w_init)
            P.wait("tensor", sxp, max(state["exp_pair"]))
            for c in range(8):
                inc = smm if c == 7 else None
                mm(ps[:, 0, c:c + 1], cc_sb[0:8, 128 * c:128 * (c + 1)],
                   sel_sb[0:8, 0:1], start=(c == 0), stop=(c == 7), inc=inc)
            P.wait("vector", smm, smm.n)
            vop("tensor_copy", n2h[:, :, 0], ps[:, 0, 0:8], inc=svc)
            n_halo_v = svc.n
        emit_attnln(g)
        if gi == 0:
            emit_boundary_send()
        if gi + 1 < G:
            pj = emit_proj(ORDER[gi + 1])
    emit_slab_stats(2)
    n_slab_stats2 = svc.n

    n_attn_vc = svc.n
    n_attn_sc = sxp.n
    n_attn_mm = smm.n
    n_attn_tp = stp.n

    P.wait("gpsimd", smm, state["proj_mm"])
    for dst, src in ((wrt, wrt_h), (wkt, wkt_h)):
        dma("gpsimd", sg, dst[:, :, :],
            bass.AP(src, 0, [[1024, 128], [128 * 1024, 8], [1, 1024]]))
    n_wffn = sg.n

    P.wait("sync", smm, n_attn_mm)
    ffn_ring = [0]
    n_tp_ring = {0: 0, 1: 0}
    n_tp_slab = {}
    for t in [0, 1, 2, 3, 4, 5, 6, 7, 12, 13, 14, 15, 8, 9, 10, 11]:
        ring = ffn_ring[0]
        nst = n_slab_stats2 if t // 4 == 2 else n_slab_stats013
        P.emit("scalar", lambda e, h=svc.h, n=nst: e.wait_ge(h, n))
        P.emit("scalar", lambda e, h=stp.h, n=n_tp_ring[ring]: e.wait_ge(h, n))
        sop("activation", n2tok[:, ring, :], x2[:, t, :], AF.Identity,
            bias=mnA[:, t:t + 1], scale=rsA[:, t:t + 1], inc=sxp)
        n_norm = sxp.n
        P.emit("sync", lambda e, h=sxp.h, n=n_norm: e.wait_ge(h, n))
        dma("sync", stp, n2t[:, :, 128 * t:128 * (t + 1)],
            n2tok[:, ring, :], transpose=True)
        n_tp_ring[ring] = stp.n
        ffn_ring[0] ^= 1
        if t % 4 == 3:
            n_tp_slab[t // 4] = stp.n
    n_n2vc = svc.n

    n_halo = n_halo_v

    P.wait("gpsimd", sg, n_wffn)

    def emit_mixes(sb_i, n_xk_free, half=None):
        rng = range(4) if half is None else (range(0, 2) if half == 0
                                             else range(2, 4))
        tbs = [(i, 4 * sb_i + i) for i in rng]
        P.wait("vector", stp, n_tp_slab[sb_i])
        if sb_i == 0:
            P.wait("vector", svc, n_halo)
        P.wait("vector", smm, n_xk_free)
        for i, t in tbs:
            c0 = 128 * t
            xap = n2t[:, :, c0:c0 + 128]
            last = i == tbs[-1][0]
            if t > 0:
                xpap = n2t[:, :, c0 - 1:c0 + 127]
                vop("tensor_sub", xr4[:, i, :, :], xap, xpap)
                vop("tensor_sub", xk4[:, i, :, :], xap, xpap,
                    inc=svc if last else None)
            else:
                vop("tensor_sub", xr4[:, i, :, 1:128], xap[:, :, 1:128],
                    n2t[:, :, 0:127])
                vop("tensor_sub", xr4[:, i, :, 0:1], xap[:, :, 0:1],
                    n2h[:, :, :])
                vop("tensor_sub", xk4[:, i, :, 1:128], xap[:, :, 1:128],
                    n2t[:, :, 0:127])
                vop("tensor_sub", xk4[:, i, :, 0:1], xap[:, :, 0:1],
                    n2h[:, :, :], inc=svc if last else None)
        P.wait("vector", svc, svc.n)
        for i, t in tbs:
            vop("tensor_mul", xr4[:, i, :, :], xr4[:, i, :, :], murep[:, :, :])
            vop("tensor_mul", xk4[:, i, :, :], xk4[:, i, :, :],
                mukrep[:, :, :], inc=svc if i == tbs[-1][0] else None)
        P.wait("vector", svc, svc.n)
        for i, t in tbs:
            c0 = 128 * t
            last = i == tbs[-1][0]
            if t > 0:
                xpap = n2t[:, :, c0 - 1:c0 + 127]
                vop("tensor_add", xr4[:, i, :, :], xr4[:, i, :, :], xpap)
                vop("tensor_add", xk4[:, i, :, :], xk4[:, i, :, :], xpap,
                    inc=svc if last else None)
            else:
                vop("tensor_add", xr4[:, i, :, 1:128], xr4[:, i, :, 1:128],
                    n2t[:, :, 0:127])
                vop("tensor_add", xr4[:, i, :, 0:1], xr4[:, i, :, 0:1],
                    n2h[:, :, :])
                vop("tensor_add", xk4[:, i, :, 1:128], xk4[:, i, :, 1:128],
                    n2t[:, :, 0:127])
                vop("tensor_add", xk4[:, i, :, 0:1], xk4[:, i, :, 0:1],
                    n2h[:, :, :], inc=svc if last else None)
        P.wait("vector", svc, svc.n)
        return svc.n

    n_mix = emit_mixes(0, smm.n)
    n_mix_h = {}
    n_sig_prev = 0
    n_vomult = {-2: 0, -1: 0}
    n_store = {-3: 0, -2: 0, -1: 0}
    n_r4 = 0
    n_kf_carry = 0
    for sb_i in range(4):
        tbs = [4 * sb_i + i for i in range(4)]
        P.wait("tensor", svc, n_mix)
        if sb_i == 0:
            P.wait("tensor", sg, n_wffn)
            P.wait("tensor", svc, n_halo)
        P.wait("tensor", svc, n_vomult[4 * sb_i - 1])
        for i in range(4):
            P.wait("tensor", sxp, n_sig_prev)
            for dc in range(8):
                for hh in range(2):
                    inc = smm if (dc == 7 and hh == 1) else None
                    mm(ps[:, hh, :], xr4[:, i, dc, :],
                       wrt[:, dc, 512 * hh:512 * (hh + 1)],
                       start=(dc == 0), stop=(dc == 7), inc=inc)
            n_r = smm.n
            P.emit("scalar", lambda e, h=smm.h, n=n_r: e.wait_ge(h, n))
            sop("activation", r4[:, i, :],
                bass.AP(ps, 0, [[8 * 512, 128], [512, 2], [1, 512]]),
                AF.Sigmoid, inc=sxp)
            n_sig_prev = sxp.n
        n_r4 = 0

        n_kf_prev = n_kf_carry
        for d2 in range(8):
            bank = 2 + (d2 % 2)
            P.wait("tensor", sxp, n_kf_prev)
            if d2 < 2:
                P.wait("tensor", svc, n_mix)
            for c2 in range(8):
                inc = smm if c2 == 7 else None
                mm(ps[:, bank, :], wkt[:, c2, 128 * d2:128 * (d2 + 1)],
                   xk4[:, :, c2, :],
                   start=(c2 == 0), stop=(c2 == 7), inc=inc)
            n_k = smm.n
            P.emit("scalar", lambda e, h=smm.h, n=n_k: e.wait_ge(h, n))
            sop("activation", krelu[:, :], ps[:, bank, :], AF.Relu, inc=sxp)
            P.emit("scalar", lambda e, h=sxp.h, n=sxp.n: e.wait_ge(h, n))
            sop("activation", kf4[:, d2, :], krelu[:, :], AF.Square, inc=sxp)
            n_kf_prev = sxp.n
        n_k_done = smm.n
        n_kf_carry = n_kf_prev

        P.wait("tensor", sxp, n_kf_prev)
        for pair in range(2):
            if sb_i + 1 < 4:
                h = emit_mixes(sb_i + 1, n_k_done, half=pair)
                if pair == 1:
                    n_mix = h
            slots = []
            for half in range(2):
                i = 2 * pair + half
                t = tbs[i]
                vslot = half
                slots.append(vslot)
                P.wait("tensor", svc, n_vomult[t - 1])
                for d2 in range(8):
                    for hh in range(2):
                        inc = smm if (d2 == 7 and hh == 1) else None
                        mm(ps[:, 4 + hh, :],
                           kf4[:, d2, 128 * i:128 * (i + 1)],
                           wvt[:, d2, 512 * hh:512 * (hh + 1)],
                           start=(d2 == 0), stop=(d2 == 7), inc=inc)
                n_vf = smm.n
                P.wait("vector", smm, n_vf)
                P.wait("vector", sxp, n_sig_prev)
                P.wait("vector", sst, n_store[t - 2])
                vop("tensor_mul", vo[:, vslot, 0:512], ps[:, 4, :],
                    r4[:, i, 0:512])
                vop("tensor_mul", vo[:, vslot, 512:1024], ps[:, 5, :],
                    r4[:, i, 512:1024], inc=svc)
                n_vomult[t] = svc.n
            P.wait("vector", svc, svc.n)
            ln_stats([(vo[:, slots[0], 0:512], vo[:, slots[0], 512:1024]),
                      (vo[:, slots[1], 0:512], vo[:, slots[1], 512:1024])])
            for half in range(2):
                i = 2 * pair + half
                t = tbs[i]
                vnorm(vo[:, slots[half], :], vo[:, slots[half], :], half,
                      inc=svc if half == 1 else None)
            P.wait("vector", svc, svc.n)
            for half in range(2):
                i = 2 * pair + half
                t = tbs[i]
                vop("tensor_add", vo[:, slots[half], :], vo[:, slots[half], :],
                    x2[:, t, :], inc=svc)
                n_out = svc.n
                P.wait("sync", svc, n_out)
                n_store[t] = dma("sync", sst,
                                 bass.AP(y_hbm, 128 * 1024 * t,
                                         [[1024, 128], [1, 1024]]),
                                 vo[:, slots[half], :])

    P.wait("sync", sst, sst.n)

    block = st.enter_context(nc.Block())
    P.run(block)
    nc._keepalive = st
    return nc


_NC_CACHE = None


def kernel(L0_attn_x, L0_attn_Mq, L0_attn_Mk, L0_attn_Mv, L0_ffn_W_r,
           L0_ffn_W_k, L0_ffn_W_v, L0_ffn_mu_r, L0_ffn_mu_k):
    global _NC_CACHE
    x = np.ascontiguousarray(np.asarray(L0_attn_x, dtype=np.float32))
    B, C, D = x.shape
    mq = np.ascontiguousarray(np.asarray(L0_attn_Mq, np.float32))
    mk = np.ascontiguousarray(np.asarray(L0_attn_Mk, np.float32))
    mv = np.ascontiguousarray(np.asarray(L0_attn_Mv, np.float32))
    wrt = np.ascontiguousarray(np.asarray(L0_ffn_W_r, np.float32).T)
    wkt = np.ascontiguousarray(np.asarray(L0_ffn_W_k, np.float32).T)
    wvt = np.ascontiguousarray(np.asarray(L0_ffn_W_v, np.float32).T)
    mur = np.ascontiguousarray(np.broadcast_to(
        np.asarray(L0_ffn_mu_r, np.float32).reshape(8, 128).T[:, :, None],
        (128, 8, 128)))
    muk = np.ascontiguousarray(np.broadcast_to(
        np.asarray(L0_ffn_mu_k, np.float32).reshape(8, 128).T[:, :, None],
        (128, 8, 128)))

    if _NC_CACHE is None:
        _NC_CACHE = build_nc()
    nc = _NC_CACHE

    in_maps = []
    for core in range(8):
        b, h = divmod(core, 2)
        sel = np.zeros((8, 1), np.float32)
        if h == 1:
            sel[core - 1, 0] = 1.0
        in_maps.append({
            "x": np.ascontiguousarray(x[b, 2048 * h:2048 * (h + 1), :]),
            "mq": mq, "mk": mk, "mv": mv,
            "wrt": wrt, "wkt": wkt, "wvt": wvt,
            "murep": mur, "mukrep": muk, "sel": sel,
        })
    res = run_bass_kernel_spmd(nc, in_maps, core_ids=list(range(8)))
    out = np.empty((B, C, D), np.float32)
    for core in range(8):
        b, h = divmod(core, 2)
        out[b, 2048 * h:2048 * (h + 1), :] = res.results[core]["y"]
    return out


# version 10
# speedup vs baseline: 1.0359x; 1.0359x over previous
import numpy as np
import concourse.bass as bass
import concourse.mybir as mybir
from concourse.bass_utils import run_bass_kernel_spmd
from contextlib import ExitStack

F32 = mybir.dt.float32
BF16 = mybir.dt.bfloat16
AF = mybir.ActivationFunctionType
ALU = mybir.AluOpType

NT = 2048
NTILE = 16
G = 4
EPS = 1e-5
SCALE = 0.125


class Sem:
    def __init__(self, nc, stack, name):
        self.h = stack.enter_context(nc.semaphore(name))
        self.n = 0

    def bump(self, k=1):
        self.n += k
        return self.n


class Prog:
    ENGS = ("sync", "gpsimd", "tensor", "scalar", "vector")

    def __init__(self):
        self.qs = {e: [] for e in self.ENGS}

    def emit(self, eng, fn):
        self.qs[eng].append(fn)

    def wait(self, eng, sem, n):
        if n > 0:
            self.emit(eng, lambda e, h=sem.h, n=n: e.wait_ge(h, n))

    def run(self, block):
        qs = self.qs

        @block.sync
        def _(e):
            for fn in qs["sync"]:
                fn(e)

        @block.gpsimd
        def _(e):
            for fn in qs["gpsimd"]:
                fn(e)

        @block.tensor
        def _(e):
            for fn in qs["tensor"]:
                fn(e)

        @block.scalar
        def _(e):
            for fn in qs["scalar"]:
                fn(e)

        @block.vector
        def _(e):
            for fn in qs["vector"]:
                fn(e)


def build_nc():
    nc = bass.Bass("TRN2", target_bir_lowering=False)
    st = ExitStack()

    x_hbm = nc.dram_tensor("x", [NT, 1024], F32, kind="ExternalInput")
    mq_h = nc.dram_tensor("mq", [1024, 512], F32, kind="ExternalInput")
    mk_h = nc.dram_tensor("mk", [1024, 512], F32, kind="ExternalInput")
    mv_h = nc.dram_tensor("mv", [1024, 1024], F32, kind="ExternalInput")
    wrt_h = nc.dram_tensor("wrt", [1024, 1024], F32, kind="ExternalInput")
    wkt_h = nc.dram_tensor("wkt", [1024, 1024], F32, kind="ExternalInput")
    wvt_h = nc.dram_tensor("wvt", [1024, 1024], F32, kind="ExternalInput")
    mur_h = nc.dram_tensor("murep", [128, 8, 128], F32, kind="ExternalInput")
    muk_h = nc.dram_tensor("mukrep", [128, 8, 128], F32, kind="ExternalInput")
    sel_h = nc.dram_tensor("sel", [8, 1], F32, kind="ExternalInput")
    y_hbm = nc.dram_tensor("y", [NT, 1024], F32, kind="ExternalOutput")
    cc_in = nc.dram_tensor("cc_in", [1, 1024], F32)
    cc_out = nc.dram_tensor("cc_out", [8, 1024], F32)

    def sb(name, shape, dtype):
        return st.enter_context(nc.sbuf_tensor(name, shape, dtype))

    x2 = sb("x2", [128, NTILE, 1024], F32)

    warena = nc.alloc_sbuf_tensor("warena", [128, 49152], mybir.dt.uint8)
    wbase = nc.lookup_mloc(warena).addr

    def at(name, shape, dtype, off):
        return nc.alloc_sbuf_tensor_at(name, shape, dtype, offset=wbase + off)

    mq = at("mq_sb", [128, 8, 512], BF16, 0)
    mk = at("mk_sb", [128, 8, 512], BF16, 8192)
    mv = at("mv_sb", [128, 8, 1024], BF16, 16384)
    wrt = at("wrt_sb", [128, 8, 1024], BF16, 0)
    wkt = at("wkt_sb", [128, 8, 1024], BF16, 16384)
    wvt = at("wvt_sb", [128, 8, 1024], BF16, 32768)

    tarena = nc.alloc_sbuf_tensor("tarena", [128, 81984], mybir.dt.uint8)
    tbase = nc.lookup_mloc(tarena).addr

    def ta(name, shape, dtype, off):
        return nc.alloc_sbuf_tensor_at(name, shape, dtype, offset=tbase + off)

    n1 = ta("n1_sb", [128, 4, 1024], BF16, 0)
    xt = ta("xt_sb", [128, 8, 512], BF16, 8192)
    qt = ta("qt_sb", [128, 4, 512], BF16, 16384)
    ktt = ta("ktt_sb", [128, 4, 512], BF16, 20480)
    ktd = ta("ktd_sb", [128, 8, 512], BF16, 24576)
    v2 = ta("v2_sb", [128, 8, 4, 128], BF16, 32768)
    e_sb = ta("e_sb", [128, 2, 4, 1024], BF16, 40960)
    dp = ta("dp_sb", [128, 2, 4], F32, 57344)
    dr = ta("dr_sb", [128, 2, 2], F32, 57472)
    lno = ta("lno_sb", [128, 4, 1024], BF16, 57600)
    n2t = ta("n2t_sb", [128, 8, 2048], BF16, 0)
    n2h = ta("n2h_sb", [128, 8, 1], BF16, 32768)
    n2tok = ta("n2tok_sb", [128, 2, 1024], BF16, 32800)
    xr4 = ta("xr4_sb", [128, 4, 8, 128], BF16, 36896)
    xk4 = ta("xk4_sb", [128, 4, 8, 128], BF16, 45088)
    t1b = ta("t1_sb", [128, 8, 128], BF16, 53280)
    r4 = ta("r4_sb", [128, 4, 1024], BF16, 55328)
    kf4 = ta("kf4_sb", [128, 8, 512], BF16, 63520)
    krelu = ta("krelu_sb", [128, 512], F32, 71712)
    vo = ta("vo_sb", [128, 2, 1024], F32, 73760)

    murep = sb("murep_sb", [128, 8, 128], BF16)
    mukrep = sb("mukrep_sb", [128, 8, 128], BF16)
    o_sb = sb("o_sb", [128, 4, 1024], BF16)
    cc_sb = sb("cc_sb", [8, 1024], BF16)
    sel_sb = sb("sel_sb", [8, 1], BF16)
    st12 = sb("st12_sb", [128, 4, 12], F32)
    ag = sb("ag_sb", [128, 4, 2, 2], F32)
    lnt = sb("lnt_sb", [128, 5, 4], F32)
    rvbuf = nc.alloc_sbuf_tensor("rvbuf", [128, 64], mybir.dt.uint8)
    rvbase = nc.lookup_mloc(rvbuf).addr
    rv_f = nc.alloc_sbuf_tensor_at("rv_f", [128, 16], F32, offset=rvbase)
    rv_i = nc.alloc_sbuf_tensor_at("rv_i", [128, 16], mybir.dt.int32,
                                   offset=rvbase)
    rsig = nc.alloc_sbuf_tensor_at("rsig_v", [128, 4], F32,
                                   offset=rvbase + 32)
    magic = sb("magic_sb", [128, 4], mybir.dt.int32)
    rtm = sb("rtm_sb", [128, 4], F32)
    rsA = sb("rsA_sb", [128, 16], F32)
    mnA = sb("mnA_sb", [128, 16], F32)

    ps = st.enter_context(nc.psum_tensor("ps", [128, 8, 512], F32))
    PAIRS = ((0, 1), (6, 7))

    sems = {}
    for name in ("sx", "sw", "stp", "ssw", "smm", "sxp", "svc", "sst", "scc",
                 "sg"):
        sems[name] = Sem(nc, st, name)
    sx, sw, stp, ssw = sems["sx"], sems["sw"], sems["stp"], sems["ssw"]
    smm, sxp, svc, sst = sems["smm"], sems["sxp"], sems["svc"], sems["sst"]
    scc, sg = sems["scc"], sems["sg"]

    P = Prog()

    def dma(eng, sem, out_ap, in_ap, transpose=False, slow=False):
        n = sem.bump(16)
        P.emit(eng, lambda e, o=out_ap, i=in_ap, h=sem.h, t=transpose, s=slow:
               e.dma_start(out=o, in_=i, transpose=t,
                           allow_slow_non_contiguous=s).then_inc(h, 16))
        return n

    def mm(out_ap, lhsT, rhs, start, stop, tp=None, inc=None):
        if inc is not None:
            n = inc.bump(1)
            P.emit("tensor", lambda e, o=out_ap, l=lhsT, r=rhs, a=start, b=stop,
                   t=tp, h=inc.h:
                   e.matmul(o, l, r, start=a, stop=b, tile_position=t,
                            skip_group_check=True).then_inc(h, 1))
            return n
        P.emit("tensor", lambda e, o=out_ap, l=lhsT, r=rhs, a=start, b=stop,
               t=tp:
               e.matmul(o, l, r, start=a, stop=b, tile_position=t,
                        skip_group_check=True))
        return None

    def vop(meth, *args, inc=None, **kw):
        if inc is not None:
            n = inc.bump(1)
            P.emit("vector", lambda e, m=meth, a=args, k=kw, h=inc.h:
                   getattr(e, m)(*a, **k).then_inc(h, 1))
            return n
        P.emit("vector", lambda e, m=meth, a=args, k=kw: getattr(e, m)(*a, **k))
        return None

    def sop(meth, *args, inc=None, **kw):
        if inc is not None:
            n = inc.bump(1)
            P.emit("scalar", lambda e, m=meth, a=args, k=kw, h=inc.h:
                   getattr(e, m)(*a, **k).then_inc(h, 1))
            return n
        P.emit("scalar", lambda e, m=meth, a=args, k=kw: getattr(e, m)(*a, **k))
        return None

    dma("sync", sx, x2[:, :, :],
        bass.AP(x_hbm, 0, [[1024, 128], [128 * 1024, NTILE], [1, 1024]]))
    for dst, src, nch, w in ((mq, mq_h, 8, 512), (mk, mk_h, 8, 512),
                             (mv, mv_h, 8, 1024)):
        dma("gpsimd", sw, dst[:, :, :],
            bass.AP(src, 0, [[w, 128], [128 * w, nch], [1, w]]))
    dma("gpsimd", sw, wvt[:, :, :],
        bass.AP(wvt_h, 0, [[1024, 128], [128 * 1024, 8], [1, 1024]]))
    dma("gpsimd", sw, murep[:, :, :], mur_h[:, :, :])
    dma("gpsimd", sw, mukrep[:, :, :], muk_h[:, :, :])
    dma("gpsimd", sw, sel_sb[:, :], sel_h[:, :])
    n_w_init = sw.n
    P.emit("vector", lambda e: e.memset(magic[:, :], 0x5F3759DF))

    def ln_stats(srcs, wait_evt=None):
        if wait_evt is not None:
            P.wait("vector", *wait_evt)
        for i, s in enumerate(srcs):
            vop("bn_stats", st12[:, i, 0:6], s[0])
            inc = svc if i == len(srcs) - 1 else None
            vop("bn_stats", st12[:, i, 6:12], s[1], inc=inc)
        P.wait("vector", svc, svc.n)
        for i in range(len(srcs)):
            vop("bn_aggr", ag[:, i, 0, :], st12[:, i, 0:6])
            inc = svc if i == len(srcs) - 1 else None
            vop("bn_aggr", ag[:, i, 1, :], st12[:, i, 6:12], inc=inc)
        P.wait("vector", svc, svc.n)
        k = len(srcs)
        vop("tensor_add", lnt[:, 0, 0:k], ag[:, 0:k, 0, 0], ag[:, 0:k, 1, 0])
        vop("tensor_sub", lnt[:, 1, 0:k], ag[:, 0:k, 0, 0], ag[:, 0:k, 1, 0])
        vop("tensor_add", lnt[:, 2, 0:k], ag[:, 0:k, 0, 1], ag[:, 0:k, 1, 1],
            inc=svc)
        P.wait("vector", svc, svc.n)
        vop("tensor_mul", lnt[:, 1, 0:k], lnt[:, 1, 0:k], lnt[:, 1, 0:k])
        vop("tensor_scalar_mul", lnt[:, 4, 0:k], lnt[:, 0, 0:k], -0.5, inc=svc)
        P.wait("vector", svc, svc.n)
        vop("scalar_tensor_tensor", lnt[:, 3, 0:k], lnt[:, 1, 0:k], 0.5,
            lnt[:, 2, 0:k], ALU.mult, ALU.add, inc=svc)
        P.wait("vector", svc, svc.n)
        vop("tensor_scalar", rv_f[:, 0:k], lnt[:, 3, 0:k], 0.5, EPS,
            ALU.mult, ALU.add, inc=svc)
        P.wait("vector", svc, svc.n)
        vop("tensor_scalar", rv_i[:, 8:8 + k], rv_i[:, 0:k], 1, None,
            ALU.arith_shift_right, inc=svc)
        P.wait("vector", svc, svc.n)
        vop("tensor_sub", rv_i[:, 8:8 + k], magic[:, 0:k], rv_i[:, 8:8 + k],
            inc=svc)
        P.wait("vector", svc, svc.n)
        for _ in range(3):
            vop("tensor_mul", rtm[:, 0:k], rsig[:, 0:k], rsig[:, 0:k],
                inc=svc)
            P.wait("vector", svc, svc.n)
            vop("tensor_mul", rtm[:, 0:k], rtm[:, 0:k], rv_f[:, 0:k],
                inc=svc)
            P.wait("vector", svc, svc.n)
            vop("tensor_scalar", rtm[:, 0:k], rtm[:, 0:k], -0.5, 1.5,
                ALU.mult, ALU.add, inc=svc)
            P.wait("vector", svc, svc.n)
            vop("tensor_mul", rsig[:, 0:k], rsig[:, 0:k], rtm[:, 0:k],
                inc=svc)
            P.wait("vector", svc, svc.n)
        return svc.n

    def vnorm(out_ap, src_ap, i, inc=None):
        return vop("tensor_scalar", out_ap, src_ap, lnt[:, 4, i:i + 1],
                   rsig[:, i:i + 1], ALU.add, ALU.mult, inc=inc)

    state = {
        "proj_mm": 0,
        "attnln_vc": 0,
        "otp": 0,
        "Bevac_vc": 0,
        "xt_tp": {},
        "exp_pair": [0, 0],
    }

    def emit_n1ln(g, first=False):
        tiles = [4 * g + i for i in range(4)]
        ln_stats([(x2[:, t, 0:512], x2[:, t, 512:1024]) for t in tiles],
                 wait_evt=(sx, 16) if first else None)
        for i in range(4):
            inc = svc if i == 3 else None
            vnorm(n1[:, i, :], x2[:, tiles[i], :], i, inc=inc)
        n_n1 = svc.n
        P.wait("sync", svc, n_n1)
        P.wait("sync", smm, state["proj_mm"])
        for i in range(4):
            dma("sync", stp, xt[:, :, 128 * i:128 * (i + 1)], n1[:, i, :],
                transpose=True)
        state["xt_tp"][g] = stp.n

    def emit_proj(g):
        P.wait("tensor", stp, state["xt_tp"][g])
        if g == 0:
            P.wait("tensor", sw, n_w_init)
        P.wait("tensor", svc, state["Bevac_vc"])
        P.wait("tensor", stp, state["otp"])
        qslots = (0, 1, 6, 7)
        if g > 0:
            P.wait("tensor", sxp, max(state["exp_pair"]))
        for jh2 in range(4):
            for dc in range(8):
                inc = smm if dc == 7 else None
                mm(ps[:, qslots[jh2], :], mq[:, dc, 128 * jh2:128 * (jh2 + 1)],
                   xt[:, dc, :], start=(dc == 0), stop=(dc == 7), inc=inc)
        n_qmm = smm.n
        for jh2 in range(4):
            for dc in range(8):
                inc = smm if dc == 7 else None
                mm(ps[:, 2 + jh2, :], mk[:, dc, 128 * jh2:128 * (jh2 + 1)],
                   xt[:, dc, :], start=(dc == 0), stop=(dc == 7), inc=inc)
        n_kmm = smm.n
        P.wait("vector", smm, n_qmm)
        for jh2 in range(4):
            vop("tensor_copy", qt[:, jh2, :], ps[:, qslots[jh2], :])
        P.wait("vector", smm, n_kmm)
        for jh2 in range(4):
            inc = svc if jh2 == 3 else None
            vop("tensor_copy", ktt[:, jh2, :], ps[:, 2 + jh2, :], inc=inc)
        P.wait("tensor", svc, svc.n)
        n_qk_ev = svc.n
        P.wait("sync", svc, n_qk_ev)
        dma("sync", ssw, bass.AP(ktd, 0, [[8 * 512, 64], [2 * 512, 4],
                                          [1, 512]]), ktt[0:64, :, :])
        dma("sync", ssw, bass.AP(ktd, 512, [[8 * 512, 64], [2 * 512, 4],
                                            [1, 512]]), ktt[64:128, :, :])
        n_swap1 = ssw.n
        P.wait("sync", ssw, n_swap1)
        dma("sync", ssw, ktd[64:128, :, :], ktd[0:64, :, :])
        n_swap = ssw.n
        for cb in range(4):
            bA, bB = (2, 3) if cb % 2 == 0 else (4, 5)
            if cb >= 2:
                P.wait("tensor", svc, svc.n)
            for dc in range(8):
                for hh, bank in ((0, bA), (1, bB)):
                    inc = smm if (dc == 7 and hh == 1) else None
                    mm(ps[:, bank, :],
                       xt[:, dc, 128 * cb:128 * (cb + 1)],
                       mv[:, dc, 512 * hh:512 * (hh + 1)],
                       start=(dc == 0), stop=(dc == 7), inc=inc)
            P.wait("vector", smm, smm.n)
            vop("tensor_copy", v2[:, 0:4, cb, :],
                bass.AP(ps, 512 * bA, [[8 * 512, 128], [128, 4], [1, 128]]))
            vop("tensor_copy", v2[:, 4:8, cb, :],
                bass.AP(ps, 512 * bB, [[8 * 512, 128], [128, 4], [1, 128]]),
                inc=svc)
        state["proj_mm"] = smm.n
        return {"swap": n_swap, "vev": svc.n}

    def sc_dual(ci, q2, jhk, cb, accum):
        pr = q2 % 2
        a, b = PAIRS[pr]
        P.wait("tensor", sxp, state["exp_pair"][pr])
        mm(ps[:, a, :], ktd[0:64, jhk, 128 * cb:128 * (cb + 1)],
           qt[0:64, q2, :], start=True, stop=True, tp=(0, 0))
        n_dual = mm(ps[:, b, :], ktd[64:128, jhk, 128 * cb:128 * (cb + 1)],
                    qt[64:128, q2, :], start=True, stop=True, tp=(64, 0),
                    inc=smm)
        P.emit("scalar", lambda e, h=smm.h, n=n_dual: e.wait_ge(h, n))
        kw = {}
        if accum:
            kw["accum_out"] = dp[:, ci % 2, q2:q2 + 1]
        sop("activation", e_sb[:, ci % 2, q2, :],
            bass.AP(ps, 512 * a, [[8 * 512, 128], [512 * (b - a), 2],
                                  [1, 512]]),
            AF.Exp, scale=SCALE, inc=sxp, **kw)
        state["exp_pair"][pr] = sxp.n

    def o_mm(pci, jhq, first, last, inc=None):
        pjhk, pcb = divmod(pci, 4)
        slot = 2 + (jhq % 4)
        return mm(ps[:, slot, :], v2[:, pjhk, pcb, :],
                  e_sb[:, pci % 2, jhq // 2,
                       512 * (jhq % 2):512 * (jhq % 2) + 512],
                  start=first, stop=last, inc=inc)

    def emit_phaseA(g, pj):
        P.wait("tensor", ssw, pj["swap"])
        P.wait("tensor", svc, pj["vev"])
        n_o_done = {}
        n_vp = {}
        first_sc_wait = True
        for ci in range(32):
            jhk, cb = divmod(ci, 4)
            for q2 in range(4):
                if first_sc_wait:
                    P.wait("scalar", stp, state["otp"])
                    first_sc_wait = False
                if ci >= 2 and q2 == 0:
                    P.wait("scalar", smm, n_o_done[ci - 2])
                sc_dual(ci, q2, jhk, cb, accum=True)
                if ci > 0:
                    if q2 == 2:
                        P.wait("tensor", svc, n_vp[ci - 1])
                    if q2 >= 2:
                        for jj in range(2):
                            jhq = 2 * (q2 - 2) + jj
                            inc = smm if jhq == 3 else None
                            o_mm(ci - 1, jhq, first=(ci - 1 == 0),
                                 last=(ci - 1 == 31), inc=inc)
                        if q2 == 3:
                            n_o_done[ci - 1] = smm.n
            P.wait("vector", sxp, state["exp_pair"][1])
            ring = ci % 2
            vop("tensor_reduce", dr[:, ring, 0:1], dp[:, ring, :],
                mybir.AxisListType.X, ALU.add, inc=svc)
            P.wait("vector", svc, svc.n)
            vop("reciprocal", dr[:, ring, 1:2], dr[:, ring, 0:1], inc=svc)
            P.wait("vector", svc, svc.n)
            vop("tensor_scalar_mul", v2[:, jhk, cb, :], v2[:, jhk, cb, :],
                dr[:, ring, 1:2], inc=svc)
            n_vp[ci] = svc.n
        P.wait("tensor", sxp, state["exp_pair"][1])
        P.wait("tensor", svc, n_vp[31])
        for jhq in range(4):
            inc = smm if jhq == 3 else None
            o_mm(31, jhq, first=False, last=True, inc=inc)
        n_oA = smm.n
        P.wait("vector", smm, n_oA)
        for jhq in range(4):
            inc = svc if jhq == 3 else None
            vop("tensor_copy",
                e_sb[:, 0, jhq // 2, 512 * (jhq % 2):512 * (jhq % 2) + 512],
                ps[:, 2 + jhq, :], inc=inc)
        n_oev = svc.n
        P.wait("sync", svc, n_oev)
        for jhq in range(4):
            dma("sync", stp, o_sb[:, :, 128 * jhq:128 * (jhq + 1)],
                e_sb[:, 0, jhq // 2,
                     512 * (jhq % 2):512 * (jhq % 2) + 512], transpose=True)
        return {"oev": n_oev, "otpA": stp.n}

    def emit_phaseB(g, pa):
        n_oB_done = {}
        for ci in range(32):
            jhk, cb = divmod(ci, 4)
            for q2 in (2, 3):
                if ci >= 2 and q2 == 2:
                    P.wait("scalar", smm, n_oB_done[ci - 2])
                sc_dual(ci, q2, jhk, cb, accum=False)
                if ci > 0:
                    if q2 == 2 and ci == 1:
                        P.wait("tensor", svc, pa["oev"])
                    for jj in range(2):
                        jhq = 4 + 2 * (q2 - 2) + jj
                        inc = smm if jhq == 7 else None
                        o_mm(ci - 1, jhq, first=(ci - 1 == 0),
                             last=(ci - 1 == 31), inc=inc)
                    if q2 == 3:
                        n_oB_done[ci - 1] = smm.n
        P.wait("tensor", sxp, state["exp_pair"][1])
        for jhq in range(4, 8):
            inc = smm if jhq == 7 else None
            o_mm(31, jhq, first=False, last=True, inc=inc)
        n_oB = smm.n
        P.wait("vector", smm, n_oB)
        P.wait("vector", stp, pa["otpA"])
        for jhq in range(4, 8):
            j = jhq - 4
            inc = svc if j == 3 else None
            vop("tensor_copy",
                e_sb[:, 0, j // 2, 512 * (j % 2):512 * (j % 2) + 512],
                ps[:, 2 + j, :], inc=inc)
        state["Bevac_vc"] = svc.n
        P.wait("sync", svc, state["Bevac_vc"])
        for jhq in range(4, 8):
            j = jhq - 4
            dma("sync", stp, o_sb[:, :, 128 * jhq:128 * (jhq + 1)],
                e_sb[:, 0, j // 2, 512 * (j % 2):512 * (j % 2) + 512],
                transpose=True)
        state["otp"] = stp.n

    def emit_attnln(g):
        tiles = [4 * g + i for i in range(4)]
        P.wait("vector", stp, state["otp"])
        P.wait("vector", scc, scc.n)
        ln_stats([(o_sb[:, i, 0:512], o_sb[:, i, 512:1024])
                  for i in range(4)])
        for i in range(4):
            inc = svc if i == 3 else None
            vnorm(lno[:, i, :], o_sb[:, i, :], i, inc=inc)
        P.wait("vector", svc, svc.n)
        for i, t in enumerate(tiles):
            inc = svc if i == 3 else None
            vop("tensor_add", x2[:, t, :], x2[:, t, :], lno[:, i, :], inc=inc)
        P.wait("vector", svc, svc.n)
        state["attnln_vc"] = svc.n

    def emit_slab_stats(si):
        slab = [4 * si + i for i in range(4)]
        ln_stats([(x2[:, t, 0:512], x2[:, t, 512:1024]) for t in slab])
        vop("tensor_mul", mnA[:, 4 * si:4 * si + 4], lnt[:, 4, 0:4],
            rsig[:, 0:4])
        vop("tensor_copy", rsA[:, 4 * si:4 * si + 4], rsig[:, 0:4], inc=svc)
        P.wait("vector", svc, svc.n)
        return svc.n

    def emit_boundary_send():
        ln_stats([(x2[:, 15, 0:512], x2[:, 15, 512:1024])])
        vnorm(lno[:, 0, :], x2[:, 15, :], 0, inc=svc)
        n_bs = svc.n
        P.emit("gpsimd", lambda e, h=svc.h, n=n_bs: e.wait_ge(h, n))
        dma("gpsimd", scc, cc_in[:, :], lno[127:128, 0, :])
        P.wait("gpsimd", scc, scc.n)
        ncc = scc.bump(1)
        P.emit("gpsimd", lambda e, h=scc.h:
               e.collective_compute(
                   "AllGather", ALU.bypass,
                   replica_groups=[list(range(8))],
                   ins=[cc_in.ap().opt()],
                   outs=[cc_out.ap().opt()]).then_inc(h, 1))
        P.wait("gpsimd", scc, ncc)
        dma("gpsimd", scc, cc_sb[:, :], cc_out[:, :])

    ORDER = [3, 0, 1, 2]
    emit_n1ln(ORDER[0], first=True)
    pj = emit_proj(ORDER[0])
    for gi, g in enumerate(ORDER):
        pa = emit_phaseA(g, pj)
        if gi + 1 < G:
            emit_n1ln(ORDER[gi + 1])
        if gi == 3:
            emit_slab_stats(0)
            emit_slab_stats(1)
            emit_slab_stats(3)
            n_slab_stats013 = svc.n
        emit_phaseB(g, pa)
        if gi == 3:
            P.wait("tensor", scc, scc.n)
            P.wait("tensor", sw, n_w_init)
            P.wait("tensor", sxp, max(state["exp_pair"]))
            for c in range(8):
                inc = smm if c == 7 else None
                mm(ps[:, 0, c:c + 1], cc_sb[0:8, 128 * c:128 * (c + 1)],
                   sel_sb[0:8, 0:1], start=(c == 0), stop=(c == 7), inc=inc)
            P.wait("vector", smm, smm.n)
            vop("tensor_copy", n2h[:, :, 0], ps[:, 0, 0:8], inc=svc)
            n_halo_v = svc.n
        emit_attnln(g)
        if gi == 0:
            emit_boundary_send()
        if gi + 1 < G:
            pj = emit_proj(ORDER[gi + 1])
    emit_slab_stats(2)
    n_slab_stats2 = svc.n

    n_attn_vc = svc.n
    n_attn_sc = sxp.n
    n_attn_mm = smm.n
    n_attn_tp = stp.n

    P.wait("gpsimd", smm, state["proj_mm"])
    for dst, src in ((wrt, wrt_h), (wkt, wkt_h)):
        dma("gpsimd", sg, dst[:, :, :],
            bass.AP(src, 0, [[1024, 128], [128 * 1024, 8], [1, 1024]]))
    n_wffn = sg.n

    P.wait("sync", smm, n_attn_mm)
    ffn_ring = [0]
    n_tp_ring = {0: 0, 1: 0}
    n_tp_slab = {}
    for t in [0, 1, 2, 3, 4, 5, 6, 7, 12, 13, 14, 15, 8, 9, 10, 11]:
        ring = ffn_ring[0]
        nst = n_slab_stats2 if t // 4 == 2 else n_slab_stats013
        P.emit("scalar", lambda e, h=svc.h, n=nst: e.wait_ge(h, n))
        P.emit("scalar", lambda e, h=stp.h, n=n_tp_ring[ring]: e.wait_ge(h, n))
        sop("activation", n2tok[:, ring, :], x2[:, t, :], AF.Identity,
            bias=mnA[:, t:t + 1], scale=rsA[:, t:t + 1], inc=sxp)
        n_norm = sxp.n
        P.emit("sync", lambda e, h=sxp.h, n=n_norm: e.wait_ge(h, n))
        dma("sync", stp, n2t[:, :, 128 * t:128 * (t + 1)],
            n2tok[:, ring, :], transpose=True)
        n_tp_ring[ring] = stp.n
        ffn_ring[0] ^= 1
        if t % 4 == 3:
            n_tp_slab[t // 4] = stp.n
    n_n2vc = svc.n

    n_halo = n_halo_v

    P.wait("gpsimd", sg, n_wffn)

    def emit_mixes(sb_i, n_xk_free, half=None):
        rng = range(4) if half is None else (range(0, 2) if half == 0
                                             else range(2, 4))
        tbs = [(i, 4 * sb_i + i) for i in rng]
        P.wait("vector", stp, n_tp_slab[sb_i])
        if sb_i == 0:
            P.wait("vector", svc, n_halo)
        P.wait("vector", smm, n_xk_free)
        for i, t in tbs:
            c0 = 128 * t
            xap = n2t[:, :, c0:c0 + 128]
            last = i == tbs[-1][0]
            if t > 0:
                xpap = n2t[:, :, c0 - 1:c0 + 127]
                vop("tensor_sub", xr4[:, i, :, :], xap, xpap)
                vop("tensor_sub", xk4[:, i, :, :], xap, xpap,
                    inc=svc if last else None)
            else:
                vop("tensor_sub", xr4[:, i, :, 1:128], xap[:, :, 1:128],
                    n2t[:, :, 0:127])
                vop("tensor_sub", xr4[:, i, :, 0:1], xap[:, :, 0:1],
                    n2h[:, :, :])
                vop("tensor_sub", xk4[:, i, :, 1:128], xap[:, :, 1:128],
                    n2t[:, :, 0:127])
                vop("tensor_sub", xk4[:, i, :, 0:1], xap[:, :, 0:1],
                    n2h[:, :, :], inc=svc if last else None)
        P.wait("vector", svc, svc.n)
        for i, t in tbs:
            vop("tensor_mul", xr4[:, i, :, :], xr4[:, i, :, :], murep[:, :, :])
            vop("tensor_mul", xk4[:, i, :, :], xk4[:, i, :, :],
                mukrep[:, :, :], inc=svc if i == tbs[-1][0] else None)
        P.wait("vector", svc, svc.n)
        for i, t in tbs:
            c0 = 128 * t
            last = i == tbs[-1][0]
            if t > 0:
                xpap = n2t[:, :, c0 - 1:c0 + 127]
                vop("tensor_add", xr4[:, i, :, :], xr4[:, i, :, :], xpap)
                vop("tensor_add", xk4[:, i, :, :], xk4[:, i, :, :], xpap,
                    inc=svc if last else None)
            else:
                vop("tensor_add", xr4[:, i, :, 1:128], xr4[:, i, :, 1:128],
                    n2t[:, :, 0:127])
                vop("tensor_add", xr4[:, i, :, 0:1], xr4[:, i, :, 0:1],
                    n2h[:, :, :])
                vop("tensor_add", xk4[:, i, :, 1:128], xk4[:, i, :, 1:128],
                    n2t[:, :, 0:127])
                vop("tensor_add", xk4[:, i, :, 0:1], xk4[:, i, :, 0:1],
                    n2h[:, :, :], inc=svc if last else None)
        P.wait("vector", svc, svc.n)
        return svc.n

    n_mix = emit_mixes(0, smm.n)
    n_mix_h = {}
    n_sig_prev = 0
    n_sig2 = [0, 0]
    n_vomult = {-2: 0, -1: 0}
    n_store = {-3: 0, -2: 0, -1: 0}
    n_r4 = 0
    n_kf_carry = 0
    for sb_i in range(4):
        tbs = [4 * sb_i + i for i in range(4)]
        P.wait("tensor", svc, n_mix)
        if sb_i == 0:
            P.wait("tensor", sg, n_wffn)
            P.wait("tensor", svc, n_halo)
            P.wait("tensor", sxp, max(state["exp_pair"]))
        P.wait("tensor", svc, n_vomult[4 * sb_i - 1])
        for i in range(4):
            pr = (0, 1) if i % 2 == 0 else (6, 7)
            P.wait("tensor", sxp, n_sig2[i % 2])
            for dc in range(8):
                for hh in range(2):
                    inc = smm if (dc == 7 and hh == 1) else None
                    mm(ps[:, pr[hh], :], xr4[:, i, dc, :],
                       wrt[:, dc, 512 * hh:512 * (hh + 1)],
                       start=(dc == 0), stop=(dc == 7), inc=inc)
            n_r = smm.n
            P.emit("scalar", lambda e, h=smm.h, n=n_r: e.wait_ge(h, n))
            sop("activation", r4[:, i, :],
                bass.AP(ps, 512 * pr[0], [[8 * 512, 128],
                                          [512 * (pr[1] - pr[0]), 2],
                                          [1, 512]]),
                AF.Sigmoid, inc=sxp)
            n_sig2[i % 2] = sxp.n
            n_sig_prev = sxp.n
        n_r4 = 0

        n_kf_prev = n_kf_carry
        for d2 in range(8):
            bank = 2 + (d2 % 2)
            P.wait("tensor", sxp, n_kf_prev)
            if d2 < 2:
                P.wait("tensor", svc, n_mix)
            for c2 in range(8):
                inc = smm if c2 == 7 else None
                mm(ps[:, bank, :], wkt[:, c2, 128 * d2:128 * (d2 + 1)],
                   xk4[:, :, c2, :],
                   start=(c2 == 0), stop=(c2 == 7), inc=inc)
            n_k = smm.n
            P.emit("scalar", lambda e, h=smm.h, n=n_k: e.wait_ge(h, n))
            sop("activation", krelu[:, :], ps[:, bank, :], AF.Relu, inc=sxp)
            P.emit("scalar", lambda e, h=sxp.h, n=sxp.n: e.wait_ge(h, n))
            sop("activation", kf4[:, d2, :], krelu[:, :], AF.Square, inc=sxp)
            n_kf_prev = sxp.n
        n_k_done = smm.n
        n_kf_carry = n_kf_prev

        P.wait("tensor", sxp, n_kf_prev)
        for pair in range(2):
            if sb_i + 1 < 4:
                h = emit_mixes(sb_i + 1, n_k_done, half=pair)
                if pair == 1:
                    n_mix = h
            slots = []
            for half in range(2):
                i = 2 * pair + half
                t = tbs[i]
                vslot = half
                slots.append(vslot)
                vb = (4, 5) if i % 2 == 0 else (2, 3)
                P.wait("tensor", svc, n_vomult[t - 2])
                if i % 2 == 1:
                    P.wait("tensor", sxp, n_kf_prev)
                for d2 in range(8):
                    for hh in range(2):
                        inc = smm if (d2 == 7 and hh == 1) else None
                        mm(ps[:, vb[hh], :],
                           kf4[:, d2, 128 * i:128 * (i + 1)],
                           wvt[:, d2, 512 * hh:512 * (hh + 1)],
                           start=(d2 == 0), stop=(d2 == 7), inc=inc)
                n_vf = smm.n
                P.wait("vector", smm, n_vf)
                P.wait("vector", sxp, n_sig_prev)
                P.wait("vector", sst, n_store[t - 2])
                vop("tensor_mul", vo[:, vslot, 0:512], ps[:, vb[0], :],
                    r4[:, i, 0:512])
                vop("tensor_mul", vo[:, vslot, 512:1024], ps[:, vb[1], :],
                    r4[:, i, 512:1024], inc=svc)
                n_vomult[t] = svc.n
            P.wait("vector", svc, svc.n)
            ln_stats([(vo[:, slots[0], 0:512], vo[:, slots[0], 512:1024]),
                      (vo[:, slots[1], 0:512], vo[:, slots[1], 512:1024])])
            for half in range(2):
                i = 2 * pair + half
                t = tbs[i]
                vnorm(vo[:, slots[half], :], vo[:, slots[half], :], half,
                      inc=svc if half == 1 else None)
            P.wait("vector", svc, svc.n)
            for half in range(2):
                i = 2 * pair + half
                t = tbs[i]
                vop("tensor_add", vo[:, slots[half], :], vo[:, slots[half], :],
                    x2[:, t, :], inc=svc)
                n_out = svc.n
                P.wait("sync", svc, n_out)
                n_store[t] = dma("sync", sst,
                                 bass.AP(y_hbm, 128 * 1024 * t,
                                         [[1024, 128], [1, 1024]]),
                                 vo[:, slots[half], :])

    P.wait("sync", sst, sst.n)

    block = st.enter_context(nc.Block())
    P.run(block)
    nc._keepalive = st
    return nc


_NC_CACHE = None


def kernel(L0_attn_x, L0_attn_Mq, L0_attn_Mk, L0_attn_Mv, L0_ffn_W_r,
           L0_ffn_W_k, L0_ffn_W_v, L0_ffn_mu_r, L0_ffn_mu_k):
    global _NC_CACHE
    x = np.ascontiguousarray(np.asarray(L0_attn_x, dtype=np.float32))
    B, C, D = x.shape
    mq = np.ascontiguousarray(np.asarray(L0_attn_Mq, np.float32))
    mk = np.ascontiguousarray(np.asarray(L0_attn_Mk, np.float32))
    mv = np.ascontiguousarray(np.asarray(L0_attn_Mv, np.float32))
    wrt = np.ascontiguousarray(np.asarray(L0_ffn_W_r, np.float32).T)
    wkt = np.ascontiguousarray(np.asarray(L0_ffn_W_k, np.float32).T)
    wvt = np.ascontiguousarray(np.asarray(L0_ffn_W_v, np.float32).T)
    mur = np.ascontiguousarray(np.broadcast_to(
        np.asarray(L0_ffn_mu_r, np.float32).reshape(8, 128).T[:, :, None],
        (128, 8, 128)))
    muk = np.ascontiguousarray(np.broadcast_to(
        np.asarray(L0_ffn_mu_k, np.float32).reshape(8, 128).T[:, :, None],
        (128, 8, 128)))

    if _NC_CACHE is None:
        _NC_CACHE = build_nc()
    nc = _NC_CACHE

    in_maps = []
    for core in range(8):
        b, h = divmod(core, 2)
        sel = np.zeros((8, 1), np.float32)
        if h == 1:
            sel[core - 1, 0] = 1.0
        in_maps.append({
            "x": np.ascontiguousarray(x[b, 2048 * h:2048 * (h + 1), :]),
            "mq": mq, "mk": mk, "mv": mv,
            "wrt": wrt, "wkt": wkt, "wvt": wvt,
            "murep": mur, "mukrep": muk, "sel": sel,
        })
    res = run_bass_kernel_spmd(nc, in_maps, core_ids=list(range(8)))
    out = np.empty((B, C, D), np.float32)
    for core in range(8):
        b, h = divmod(core, 2)
        out[b, 2048 * h:2048 * (h + 1), :] = res.results[core]["y"]
    return out
